# version 06
# baseline (speedup 1.0000x reference)
import numpy as np
import ml_dtypes

import concourse.bass as bass
import concourse.mybir as mybir
from concourse.bass_utils import run_bass_kernel_spmd

F32 = mybir.dt.float32
BF16 = mybir.dt.bfloat16
I16 = mybir.dt.int16
AF = mybir.ActivationFunctionType
ALU = mybir.AluOpType
AXX = mybir.AxisListType.X

NCORES = 8
D = 4096
NH = 32
HD = 128
TOK = 1024
TT = 512
NTILE = TOK // TT
NTAU = TT // 128
I_REAL = 11008
I_PAD = 11264
ICH = I_PAD // 128
ICH_REAL = I_REAL // 128
SH_I = I_PAD // NCORES
SH_CH = ICH // NCORES

_NF4 = np.array([
    -1.0, -0.6961928009986877, -0.5250730514526367, -0.39491748809814453,
    -0.28444138169288635, -0.18477343022823334, -0.09105003625154495, 0.0,
    0.07958029955625534, 0.16093020141124725, 0.24611230194568634,
    0.33791524171829224, 0.44070982933044434, 0.5626170039176941,
    0.7229568362236023, 1.0], dtype=np.float64)
_SL = np.diff(_NF4)
_S0 = np.float32(_SL[0])
_GAM = _SL[1:] - _SL[:-1]
_GAM32 = np.float32(np.abs(_GAM))
_GSIGN = np.sign(_GAM)
_T0 = np.float32(_NF4[0])

_INV_SQRT_D = np.float32(1.0 / np.sqrt(128.0))
_THR_RAW = np.float32(np.float64(0.001) / np.float64(_INV_SQRT_D))
_EPS = np.float32(1e-5)


ENGINES = ("tensor", "vector", "scalar", "sync", "gpsimd")


class _K:
    def __init__(self, nc):
        self.nc = nc
        self.streams = {e: [] for e in ENGINES}
        self.sem_handles = {}
        self.sem_counts = {}
        self.waited = {e: {} for e in ENGINES}
        self.dma_idx = {"sync": 0, "scalar": 0, "gpsimd": 0}
        self._ctx = None

    def sem(self, name):
        h = self._ctx.enter_context(self.nc.semaphore(name))
        self.sem_handles[name] = h
        self.sem_counts[name] = 0
        return name

    def emit(self, eng, deps, build, inc_sem, inc_amt):
        merged = {}
        for s, v in deps:
            if v > merged.get(s, 0):
                merged[s] = v
        waits = []
        w = self.waited[eng]
        for s, v in sorted(merged.items()):
            assert v <= self.sem_counts[s], f"future token {s}={v}"
            if w.get(s, -1) >= v:
                continue
            waits.append((s, v))
            w[s] = v
        self.sem_counts[inc_sem] += inc_amt
        tok = (inc_sem, self.sem_counts[inc_sem])
        self.streams[eng].append((tuple(waits), build, self.sem_handles[inc_sem], inc_amt))
        return tok

    def _eng_sem(self, eng):
        return {"tensor": "sT", "vector": "sV", "scalar": "sS",
                "sync": "sQ", "gpsimd": "sP"}[eng]

    def op(self, eng, deps, build):
        return self.emit(eng, deps, build, self._eng_sem(eng), 1)

    NLANE = 8

    def dma(self, eng, deps, build):
        base = {"sync": "sQ", "scalar": "sR", "gpsimd": "sG"}[eng]
        idx = self.dma_idx[eng]
        self.dma_idx[eng] = idx + 1
        lane = f"{base}{idx % self.NLANE}"
        deps = list(deps) + [(lane, self.sem_counts[lane])]
        return self.emit(eng, deps, build, lane, 16)

    def queue_fence(self, eng):
        base = {"sync": "sQ", "scalar": "sR", "gpsimd": "sG"}[eng]
        return [(f"{base}{i}", self.sem_counts[f"{base}{i}"]) for i in range(self.NLANE)]

    def final_token(self, sem):
        return (sem, self.sem_counts[sem])


class _Slot:
    __slots__ = ("w", "r")

    def __init__(self):
        self.w = []
        self.r = []

    def rdep(self):
        return list(self.w)

    def wdep(self):
        return list(self.r) + list(self.w)

    def read(self, tok):
        self.r.append(tok)

    def wrote(self, tok):
        self.w = list(tok) if isinstance(tok, list) else [tok]
        self.r = []


def _build(phases=("p1", "p2", "b0", "c0", "b1", "c1")):
    from contextlib import ExitStack
    nc = bass.Bass("TRN2", target_bir_lowering=False, debug=False)

    dram = lambda n, sh, dt, k="ExternalInput", **kw: nc.dram_tensor(n, sh, dt, kind=k, **kw).ap()

    x_in = dram("x_in", [TOK, D], F32)
    wq_p = dram("wq_p", [4, 128, D], F32)
    wk_p = dram("wk_p", [4, 128, D], F32)
    wv_p = dram("wv_p", [4, 128, D], F32)
    wo_p = dram("wo_p", [4, 128, D], F32)
    bqkv = dram("bqkv", [128, 96], F32)
    bo_in = dram("bo_in", [1, D], F32)
    cg_in = dram("cg_in", [SH_I, D], BF16)
    cu_in = dram("cu_in", [SH_I, D], BF16)
    cd_in = dram("cd_in", [D, SH_I], BF16)
    sg_in = dram("sg_in", [64, SH_I], F32)
    su_in = dram("su_in", [64, SH_I], F32)
    sd_in = dram("sd_in", [22, D], F32)
    ident_in = dram("ident_in", [128, 128], BF16)
    bthr_in = dram("bthr_in", [128, 128], F32)
    pwlb_in = dram("pwlb_in", [128, 16], F32)
    ones_in = dram("ones_in", [1, 128], BF16)
    y_out = dram("y_out", [TOK, D], F32, k="ExternalOutput")

    r1_hbm = dram("r1_hbm", [NTILE * TT, D], F32, k="Internal")
    c_q = dram("c_q", [4 * 128, D], BF16, k="Internal")
    c_k = dram("c_k", [4 * 128, D], BF16, k="Internal")
    c_v = dram("c_v", [4 * 128, D], BF16, k="Internal")
    c_o = dram("c_o", [4 * 128, D], BF16, k="Internal")
    f_q = dram("f_q", [NH * 128, D], BF16, k="Internal")
    f_k = dram("f_k", [NH * 128, D], BF16, k="Internal")
    f_v = dram("f_v", [NH * 128, D], BF16, k="Internal")
    f_o = dram("f_o", [NH * 128, D], BF16, k="Internal")
    c_g = dram("c_g", [SH_CH * 128, D], BF16, k="Internal")
    c_u = dram("c_u", [SH_CH * 128, D], BF16, k="Internal")
    c_d = dram("c_d", [SH_CH * 128, D], BF16, k="Internal")
    f_g = dram("f_g", [ICH * 128, D], BF16, k="Internal")
    f_u = dram("f_u", [ICH * 128, D], BF16, k="Internal")
    f_d = dram("f_d", [ICH * 128, D], BF16, k="Internal")

    with ExitStack() as ctx:
        E = ctx.enter_context
        t_xnT = E(nc.sbuf_tensor("t_xnT", [128, 16384], BF16))
        t_qkv = E(nc.sbuf_tensor("t_qkv", [128, 49152], BF16))
        t_oT = E(nc.sbuf_tensor("t_oT", [128, 16384], BF16))
        t_x = E(nc.sbuf_tensor("t_x", [128, 4096], F32))
        t_w = E(nc.sbuf_tensor("t_w", [128, 8192], BF16))
        ident = E(nc.sbuf_tensor("ident", [128, 128], BF16))
        bthr = E(nc.sbuf_tensor("bthr", [128, 128], F32))
        pwlb = E(nc.sbuf_tensor("pwlb", [128, 16], F32))
        bqkv_sb = E(nc.sbuf_tensor("bqkv_sb", [128, 96], F32))
        bo_bf = E(nc.sbuf_tensor("bo_bf", [1, D], BF16))
        ones_sb = E(nc.sbuf_tensor("ones_sb", [1, 128], BF16))
        e_t = [E(nc.sbuf_tensor(f"e_t{p}", [128, 128], BF16)) for p in range(2)]
        mb_t = [E(nc.sbuf_tensor(f"mb_t{p}", [128, 128], BF16)) for p in range(2)]
        vs_t = [E(nc.sbuf_tensor(f"vs_t{p}", [128, 132], BF16)) for p in range(2)]
        o4_t = [E(nc.sbuf_tensor(f"o4_t{p}", [128, 128], BF16)) for p in range(2)]
        zr_t = [E(nc.sbuf_tensor(f"zr_t{p}", [128, 1], F32)) for p in range(2)]
        stat = E(nc.sbuf_tensor("stat", [128, 32], F32))
        t_dummy = E(nc.sbuf_tensor("t_dummy", [1, 8], F32))
        pb = [E(nc.psum_tensor(f"pb{i}", [128, 512], F32)) for i in range(8)]

        K = _K(nc)
        K._ctx = ctx
        for s in ("sT", "sV", "sS", "sP"):
            K.sem(s)
        for base in ("sQ", "sR", "sG"):
            for i in range(_K.NLANE):
                K.sem(f"{base}{i}")
        cc_sems = {m: K.sem(f"cc_{m}") for m in ("q", "k", "v", "o", "g", "u", "d")}

        qkv_f32 = t_qkv.ap().bitcast(F32)
        pb_bf = [p.ap().bitcast(BF16) for p in pb]

        S_pb = [_Slot() for _ in range(8)]
        S_xnT = _Slot()
        S_q, S_k, S_v = _Slot(), _Slot(), _Slot()
        S_oT = _Slot()
        S_x = [_Slot() for _ in range(8)]
        S_w = [_Slot() for _ in range(8)]
        S_e = [_Slot() for _ in range(2)]
        S_mb = [_Slot() for _ in range(2)]
        S_vs = [_Slot() for _ in range(2)]
        S_o4 = [_Slot() for _ in range(2)]
        S_zr = [_Slot() for _ in range(2)]
        S_stats = [_Slot() for _ in range(NTAU)]
        S_inter = _Slot()
        S_xtile = _Slot()
        S_xtile2 = _Slot()
        S_deq = [_Slot() for _ in range(6)]
        S_sop = _Slot()
        S_dout = _Slot()
        S_r1hbm = [[_Slot() for _ in range(NTAU)] for _ in range(NTILE)]
        S_bo = _Slot()

        S_dummy = _Slot()

        def act_release(bank, tok):
            tt_ = K.op("vector", [tok] + S_dummy.wdep(),
                       lambda e, bank=bank: e.tensor_copy(
                           t_dummy.ap()[0:1, 0:1], pb[bank].ap()[0:1, 0:1]))
            S_dummy.wrote(tt_)
            S_pb[bank].read(tt_)
            return tt_

        tok0 = []

        t = K.dma("sync", [], lambda e: e.dma_start(out=ident[:], in_=ident_in))
        ident_tok = [t]
        t = K.dma("sync", [], lambda e: e.dma_start(out=bthr[:], in_=bthr_in))
        bthr_tok = [t]
        t = K.dma("sync", [], lambda e: e.dma_start(out=pwlb[:], in_=pwlb_in))
        pwlb_tok = [t]
        t = K.dma("sync", [], lambda e: e.dma_start(out=bqkv_sb[:], in_=bqkv))
        bqkv_tok = [t]
        t = K.dma("gpsimd", [], lambda e: e.dma_start(out=bo_bf[:], in_=bo_in))
        S_bo.wrote(t)
        t = K.dma("sync", [], lambda e: e.dma_start(out=ones_sb[:], in_=ones_in))
        ones_tok = [t]
        for p in range(2):
            t = K.op("vector", [], lambda e, p=p: e.memset(vs_t[p].ap()[:, 128:129], 1.0))
            S_vs[p].wrote(t)

        for m, (wp, cm, name) in enumerate([] if "p1" not in phases else
            [(wq_p, c_q, "q"), (wk_p, c_k, "k"), (wv_p, c_v, "v"), (wo_p, c_o, "o")]
        ):
            dtoks = []
            for b in range(4):
                t = K.dma("gpsimd", [], lambda e, wp=wp, cm=cm, b=b: e.dma_start(
                    out=cm.rearrange("(b p) d -> b p d", b=4)[b], in_=wp[b]))
                dtoks.append(t)
            full = {"q": f_q, "k": f_k, "v": f_v, "o": f_o}[name]
            K.emit("gpsimd", dtoks, lambda e, cm=cm, full=full: e.collective_compute(
                "AllGather", ALU.bypass, replica_groups=[list(range(NCORES))],
                ins=[cm.opt()], outs=[full.opt()]), cc_sems[name], 1)
        AG = {m: (cc_sems[m], 1) for m in ("q", "k", "v", "o")}

        J_PADD = (1, 2)
        J_VADD = tuple(j for j in range(3, 15))
        _G = {j: float(_GAM32[j - 1]) for j in range(1, 15)}
        _SGN = {j: (_GSIGN[j - 1] > 0) for j in range(1, 15)}
        _CT = float(np.float32(np.float64(_T0) / np.float64(_S0)))

        SD_ct = [_Slot() for _ in range(2)]
        SD_v = [_Slot() for _ in range(2)]
        SD_aV = [_Slot() for _ in range(2)]
        SD_t = [_Slot() for _ in range(2)]
        SD_sop = [_Slot() for _ in range(2)]
        SD_out = [_Slot() for _ in range(2)]

        def _deq_all_trackers():
            out = []
            for sl in (SD_ct + SD_v + SD_aV + SD_t + SD_sop + SD_out):
                out += sl.wdep()
            return out

        def deq_views(pi, W):
            f = qkv_f32
            v_ = f[:, pi * 3 * W:pi * 3 * W + W]
            aV = f[:, pi * 3 * W + W:pi * 3 * W + 2 * W]
            sop = f[:, pi * 3 * W + 2 * W:pi * 3 * W + 3 * W]
            tt = [f[:, 6 * W + i * W:6 * W + (i + 1) * W] for i in range(2)]
            return v_, aV, sop, tt

        def pwl_tile(pi, ci, oi, ct_ap, W, sop_ready, ct_ready, out_ap, fence):
            v_, aV, sop, tt = deq_views(pi, W)
            tv = K.op("vector", ct_ready + SD_v[pi].wdep() + fence,
                      lambda e, v_=v_, ct_ap=ct_ap: e.tensor_copy(v_, ct_ap.bitcast(I16)))
            SD_v[pi].wrote(tv)
            SD_ct[ci].read(tv)
            acc_tok = K.op("vector", [tv] + SD_aV[pi].wdep(),
                           lambda e, v_=v_, aV=aV: e.tensor_scalar(
                               aV, v_, _CT, float(_S0), ALU.add, ALU.mult))
            SD_aV[pi].wrote(acc_tok)
            SD_v[pi].read(acc_tok)
            for k, j in enumerate(list(J_PADD) + list(J_VADD)):
                sl = k & 1
                tterm = K.op("scalar", [tv] + SD_t[sl].wdep() + pwlb_tok,
                             lambda e, j=j, sl=sl, v_=v_, tt=tt: e.activation(
                                 tt[sl], v_, AF.Relu, scale=_G[j],
                                 bias=pwlb[:, j:j + 1]))
                SD_t[sl].wrote(tterm)
                SD_v[pi].read(tterm)
                opa = ALU.add if _SGN[j] else ALU.subtract
                eng = "gpsimd" if j in J_PADD else "vector"
                acc_tok = K.op(eng, [tterm, acc_tok],
                               lambda e, sl=sl, opa=opa, aV=aV, tt=tt: e.tensor_tensor(
                                   aV, aV, tt[sl], opa))
                SD_t[sl].read(acc_tok)
                SD_aV[pi].wrote(acc_tok)
            to = K.op("vector", [acc_tok] + sop_ready + SD_out[oi].wdep(),
                      lambda e, out_ap=out_ap, aV=aV, sop=sop: e.tensor_tensor(
                          out_ap, aV, sop, ALU.mult))
            SD_out[oi].r.append(to)
            SD_sop[pi].read(to)
            SD_aV[pi].read(to)
            return to

        for mi, (cin, sin, cout, agn) in enumerate(
            [] if "p2" not in phases else
            [(cg_in, sg_in, c_g, "g"), (cu_in, su_in, c_u, "u"), (cd_in, sd_in, c_d, "d")]
        ):
            store_toks = []
            mat_fence = _deq_all_trackers()
            full = {"g": f_g, "u": f_u, "d": f_d}[agn]
            if agn != "d":
                W = 2 * SH_I
                for m in range(16):
                    pi = m & 1
                    j0 = 2 * m
                    ct_ap = t_xnT.ap()[:, pi * W:(pi + 1) * W]
                    tc_toks = []
                    for h in range(2):
                        tc = K.dma("sync", SD_ct[pi].wdep() +
                                   (mat_fence if m < 2 else []),
                                   lambda e, cin=cin, j=j0 + h, ct_ap=ct_ap, h=h:
                                   e.dma_start(
                                       out=ct_ap[:, h * SH_I:(h + 1) * SH_I],
                                       in_=cin[:, j * 128:(j + 1) * 128],
                                       transpose=True))
                        tc_toks.append(tc)
                    SD_ct[pi].wrote(tc_toks[-1])
                    SD_ct[pi].r.append(tc_toks[0])
                    views = deq_views(pi, W)
                    sop = views[2]
                    ts_toks = []
                    for h in range(2):
                        ssrc = sin[2 * (j0 + h):2 * (j0 + h) + 2].unsqueeze(
                            1).broadcast_to([2, 64, SH_I])
                        ts = K.dma("scalar", SD_sop[pi].wdep() +
                                   (mat_fence if m < 2 else []),
                                   lambda e, ssrc=ssrc, sop=sop, h=h: e.dma_start(
                                       out=sop[:, h * SH_I:(h + 1) * SH_I], in_=ssrc))
                        ts_toks.append(ts)
                    SD_sop[pi].wrote(ts_toks[-1])
                    SD_sop[pi].r.append(ts_toks[0])
                    out_ap = t_oT.ap()[:, pi * W:(pi + 1) * W]
                    to = pwl_tile(pi, pi, pi, ct_ap, W, ts_toks, tc_toks, out_ap,
                                  mat_fence if m < 2 else [])
                    for h in range(2):
                        dst = cout.rearrange("(a p) d -> a p d", p=128).transpose(
                            [1, 0, 2])[:, :, (j0 + h) * 128:(j0 + h + 1) * 128]
                        src2 = out_ap[:, h * SH_I:(h + 1) * SH_I].rearrange(
                            "p (a i) -> p a i", a=SH_CH)
                        tst = K.dma("gpsimd", [to],
                                    lambda e, dst=dst, src2=src2: e.dma_start(
                                        out=dst, in_=src2))
                        SD_out[pi].read(tst)
                        store_toks.append(tst)
            else:
                W = 2048
                for a in range(SH_CH):
                    ca = a & 1
                    ct_a = t_xnT.ap()[:, ca * 4096:(ca + 1) * 4096]
                    tc = K.dma("sync", SD_ct[ca].wdep() +
                               (mat_fence if a < 2 else []),
                               lambda e, cin=cin, a=a, ct_a=ct_a: e.dma_start(
                                   out=ct_a, in_=cin[:, a * 128:(a + 1) * 128],
                                   transpose=True))
                    SD_ct[ca].wrote(tc)
                    out_a = t_oT.ap()[:, ca * 4096:(ca + 1) * 4096]
                    sub_toks = []
                    for sub in range(2):
                        pi = sub & 1
                        ct_ap = ct_a[:, sub * W:(sub + 1) * W]
                        views = deq_views(pi, W)
                        sop = views[2]
                        ssrc = sin[2 * a:2 * a + 2, sub * W:(sub + 1) * W].unsqueeze(
                            1).broadcast_to([2, 64, W])
                        ts = K.dma("scalar", SD_sop[pi].wdep() +
                                   (mat_fence if a < 1 else []),
                                   lambda e, ssrc=ssrc, sop=sop: e.dma_start(
                                       out=sop, in_=ssrc))
                        SD_sop[pi].wrote(ts)
                        out_ap = out_a[:, sub * W:(sub + 1) * W]
                        to = pwl_tile(pi, ca, ca, ct_ap, W, [ts], [tc], out_ap,
                                      mat_fence if a < 1 else [])
                        sub_toks.append(to)
                    dst = cout.rearrange("(a p) d -> a p d", p=128)[a]
                    tst = K.dma("gpsimd", sub_toks,
                                lambda e, dst=dst, out_a=out_a: e.dma_start(
                                    out=dst, in_=out_a))
                    SD_out[ca].read(tst)
                    store_toks.append(tst)
            K.emit("gpsimd", store_toks, lambda e, cout=cout, full=full: e.collective_compute(
                "AllGather", ALU.bypass, replica_groups=[list(range(NCORES))],
                ins=[cout.opt()], outs=[full.opt()]), cc_sems[agn], 1)
            AG[agn] = (cc_sems[agn], 1)

        phase_tok = {"deq_done": K.final_token("sV")}

        def ln_and_transpose(ti, which):
            out_toks = []
            for tau in range(NTAU):
                src = x_in[ti * TT + tau * 128: ti * TT + (tau + 1) * 128, :] if which == "ln1" \
                    else r1_hbm[ti * TT + tau * 128: ti * TT + (tau + 1) * 128, :]
                xb = t_x[:] if (tau & 1) == 0 else qkv_f32[:, 20480:24576]
                xs_t = S_xtile if (tau & 1) == 0 else S_xtile2
                dep = xs_t.wdep() + ([t for sl in S_x for t in sl.wdep()]
                                     if (tau & 1) == 0 else S_v.wdep() + S_inter.wdep())
                if which == "ln2":
                    dep += S_r1hbm[ti][tau].rdep()
                tl = K.dma("scalar", dep, lambda e, src=src, xb=xb: e.dma_start(out=xb, in_=src))
                xs_t.wrote(tl)
                if which == "ln2":
                    S_r1hbm[ti][tau].read(tl)
                sb_ = tau * 8
                st_ = S_stats[tau]
                tr_ = K.op("vector", [tl] + st_.wdep(),
                           lambda e, sb_=sb_, xb=xb: e.reduce_sum(
                               stat.ap()[:, sb_:sb_ + 1], xb, AXX))
                dump = qkv_f32[:, 2048 + 4096 * tau:2048 + 4096 * (tau + 1)]
                tq = K.op("scalar", [tl] + st_.wdep(),
                          lambda e, sb_=sb_, dump=dump, xb=xb: e.activation(
                              dump, xb, AF.Square, accum_out=stat.ap()[:, sb_ + 1:sb_ + 2]))
                tm = K.op("vector", [tr_], lambda e, sb_=sb_: e.tensor_scalar_mul(
                    stat.ap()[:, sb_ + 2:sb_ + 3], stat.ap()[:, sb_:sb_ + 1], 1.0 / D))
                tmsq = K.op("scalar", [tm], lambda e, sb_=sb_: e.activation(
                    stat.ap()[:, sb_ + 3:sb_ + 4], stat.ap()[:, sb_ + 2:sb_ + 3], AF.Square))
                tvar = K.op("vector", [tmsq, tq], lambda e, sb_=sb_: e.scalar_tensor_tensor(
                    stat.ap()[:, sb_ + 4:sb_ + 5], stat.ap()[:, sb_ + 1:sb_ + 2], 1.0 / D,
                    stat.ap()[:, sb_ + 3:sb_ + 4], ALU.mult, ALU.subtract))
                tsd = K.op("scalar", [tvar] + pwlb_tok, lambda e, sb_=sb_: e.activation(
                    stat.ap()[:, sb_ + 5:sb_ + 6], stat.ap()[:, sb_ + 4:sb_ + 5],
                    AF.Sqrt, bias=pwlb[:, 15:16]))
                trs = K.op("vector", [tsd], lambda e, sb_=sb_: e.reciprocal(
                    stat.ap()[:, sb_ + 6:sb_ + 7], stat.ap()[:, sb_ + 5:sb_ + 6]))
                st_.wrote(trs)
                xn_nat = t_qkv.ap()[:, 0:4096]
                tapp = K.op("vector", [trs, tl] + S_q.wdep(),
                            lambda e, sb_=sb_, xb=xb: e.tensor_scalar(
                                xn_nat, xb, stat.ap()[:, sb_ + 2:sb_ + 3],
                                stat.ap()[:, sb_ + 6:sb_ + 7],
                                ALU.subtract, ALU.mult))
                S_q.wrote(tapp)
                st_.read(tapp)
                xs_t.read(tapp)
                for j in range(NH):
                    bk = 4 + (j & 3)
                    tt_ = K.op("tensor", [tapp] + S_pb[bk].wdep() + ident_tok,
                               lambda e, j=j, bk=bk: e.transpose(
                                   pb_bf[bk][:, 0:128], t_qkv.ap()[:, j * 128:(j + 1) * 128],
                                   ident[:]))
                    S_pb[bk].wrote(tt_)
                    S_q.read(tt_)
                    tcp = K.op("vector", [tt_] + S_xnT.wdep(),
                               lambda e, j=j, tau=tau, bk=bk: e.tensor_copy(
                                   t_xnT.ap()[:, j * 512 + tau * 128: j * 512 + tau * 128 + 128],
                                   pb_bf[bk][:, 0:128]))
                    S_pb[bk].read(tcp)
                    out_toks.append(tcp)
            S_xnT.wrote(out_toks[-1])
            for tk in out_toks[:-1]:
                S_xnT.r.append(tk)
            return out_toks

        def attn_tile(ti, level=4):
            xnT_toks = ln_and_transpose(ti, "ln1")
            if level < 2:
                return
            for mi, (full, Sdst, bcol) in enumerate(
                [(f_q, S_q, 0), (f_k, S_k, 32), (f_v, S_v, 64)]
            ):
                for h in range(NH):
                    ws = h & 1
                    wap = t_w.ap()[:, ws * 4096:(ws + 1) * 4096]
                    wdep = S_w[2 * ws].wdep() + S_w[2 * ws + 1].wdep() + [AG[("q", "k", "v")[mi]]]
                    td = K.dma("sync", wdep, lambda e, full=full, h=h, wap=wap: e.dma_start(
                        out=wap, in_=full.rearrange("(b p) d -> b p d", p=128)[h]))
                    S_w[2 * ws].wrote(td)
                    S_w[2 * ws + 1].wrote(td)
                    bk = (mi * NH + h) & 3
                    def chain(e, wap=wap, bk=bk):
                        last = None
                        for j in range(32):
                            last = e.matmul(pb[bk][:], wap[:, j * 128:(j + 1) * 128],
                                            t_xnT.ap()[:, j * 512:(j + 1) * 512],
                                            start=(j == 0), stop=(j == 31))
                        return last
                    tc = K.op("tensor", [td] + xnT_toks[-1:] + S_pb[bk].wdep(), chain)
                    S_pb[bk].wrote(tc)
                    S_w[2 * ws].read(tc)
                    S_w[2 * ws + 1].read(tc)
                    S_xnT.read(tc)
                    dst = t_qkv.ap()[:, mi * 16384:(mi + 1) * 16384].rearrange(
                        "p (t g) -> p t g", g=32)[:, :, h:h + 1].squeeze(2)
                    tev = K.op("scalar", [tc] + bqkv_tok + Sdst.wdep(),
                               lambda e, dst=dst, bk=bk, bcol=bcol, h=h: e.activation(
                                   dst, pb[bk][:], AF.Identity,
                                   bias=bqkv_sb[:, bcol + h:bcol + h + 1]))
                    S_pb[bk].read(tev)
                    act_release(bk, tev)
                    Sdst.wrote(tev) if h == NH - 1 else Sdst.r.append(tev)
            if level < 3:
                return
            import os
            b3ops = int(os.environ.get("B3OPS", "15"))
            ngroups = TT // 4 if level < 30 else (level - 30)
            qr = S_q.w and [S_q.w] or []
            qkv_ready = S_q.rdep() + S_k.rdep() + S_v.rdep()
            for gg in range(ngroups):
                p = gg & 1
                bsc, bav, bvt, bot = 0 + p, 2 + p, 4 + p, 6 + p
                if not (b3ops & 1):
                    continue
                tvt = K.op("tensor", qkv_ready + S_pb[bvt].wdep() + ident_tok,
                           lambda e, gg=gg, bvt=bvt: e.transpose(
                               pb_bf[bvt][:, 0:128],
                               t_qkv.ap()[:, 32768 + gg * 128:32768 + (gg + 1) * 128],
                               ident[:]))
                S_pb[bvt].wrote(tvt)
                S_v.read(tvt)
                tvs = K.op("vector", [tvt] + S_vs[p].wdep(),
                           lambda e, p=p, bvt=bvt: e.tensor_copy(
                               vs_t[p].ap()[:, 0:128], pb_bf[bvt][:, 0:128]))
                S_pb[bvt].read(tvs)
                S_vs[p].r.append(tvs)
                vs_ready = tvs
                if not (b3ops & 2):
                    continue
                tsc = K.op("tensor", qkv_ready + S_pb[bsc].wdep(),
                           lambda e, gg=gg, bsc=bsc: e.matmul(
                               pb[bsc][:, 0:128],
                               t_qkv.ap()[:, 16384 + gg * 128:16384 + (gg + 1) * 128],
                               t_qkv.ap()[:, gg * 128:(gg + 1) * 128],
                               start=True, stop=True))
                S_pb[bsc].wrote(tsc)
                S_k.read(tsc)
                S_q.read(tsc)
                tex = K.op("scalar", [tsc] + S_e[p].wdep(),
                           lambda e, p=p, bsc=bsc: e.activation(
                               e_t[p][:], pb[bsc][:, 0:128], AF.Exp, scale=float(_INV_SQRT_D)))
                S_e[p].wrote(tex)
                S_pb[bsc].read(tex)
                tmb = K.op("vector", [tsc, tex] + bthr_tok + S_mb[p].wdep(),
                           lambda e, p=p, bsc=bsc: e.tensor_tensor(
                               mb_t[p][:], pb[bsc][:, 0:128], bthr[:], ALU.is_gt))
                S_mb[p].wrote(tmb)
                S_pb[bsc].read(tmb)
                tebd = K.op("vector", [tmb, tex],
                            lambda e, p=p: e.tensor_tensor(e_t[p][:], e_t[p][:], mb_t[p][:],
                                                           ALU.mult))
                S_e[p].wrote(tebd)
                S_mb[p].read(tebd)
                if not (b3ops & 4):
                    continue
                tav = K.op("tensor", [tebd, vs_ready] + S_pb[bav].wdep() + S_vs[p].rdep(),
                           lambda e, p=p, bav=bav: e.matmul(
                               pb[bav][:, 0:129], e_t[p][:], vs_t[p].ap()[:, 0:129],
                               start=True, stop=True))
                S_pb[bav].wrote(tav)
                S_e[p].read(tav)
                S_vs[p].r.append(tav)
                tzr = K.op("vector", [tav] + S_zr[p].wdep(),
                           lambda e, p=p, bav=bav: e.reciprocal(zr_t[p][:], pb[bav][:, 128:129]))
                S_zr[p].wrote(tzr)
                S_pb[bav].read(tzr)
                to4 = K.op("scalar", [tzr] + S_o4[p].wdep(),
                           lambda e, p=p, bav=bav: e.activation(
                               o4_t[p][:], pb[bav][:, 0:128], AF.Identity,
                               scale=zr_t[p].ap()[:, 0:1]))
                S_o4[p].wrote(to4)
                S_pb[bav].read(to4)
                act_release(bav, to4)
                S_zr[p].read(to4)
                if not (b3ops & 8):
                    continue
                tot = K.op("tensor", [to4] + S_pb[bot].wdep() + ident_tok,
                           lambda e, p=p, bot=bot: e.transpose(
                               pb_bf[bot][:, 0:128], o4_t[p][:], ident[:]))
                S_pb[bot].wrote(tot)
                S_o4[p].read(tot)
                dst = t_oT.ap().rearrange("p (h t) -> p h t", t=512)[:, :, 4 * gg:4 * gg + 4]
                srcp = pb_bf[bot][:, 0:128].rearrange("p (t h) -> p h t", h=32)
                tcp = K.op("vector",
                           [tot] + (S_oT.wdep() + SD_out[0].wdep() + SD_out[1].wdep()
                                    if gg == 0 else []),
                           lambda e, dst=dst, srcp=srcp: e.tensor_copy(dst, srcp))
                S_pb[bot].read(tcp)
                S_oT.r.append(tcp)
            if S_oT.r:
                S_oT.wrote(S_oT.r[-1])
            oT_ready = S_oT.rdep()

            if level < 4:
                return
            for cq in range(4):
                bank_first_deps = []
                for b in range(8):
                    bank_first_deps += S_pb[b].wdep()
                    S_pb[b].r = []
                    S_pb[b].w = None
                for phi in range(NH):
                    ws = phi & 1
                    wap = t_w.ap()[:, ws * 1024:(ws + 1) * 1024]
                    td = K.dma("sync", S_w[ws].wdep() + [AG["o"]],
                               lambda e, phi=phi, cq=cq, wap=wap: e.dma_start(
                                   out=wap,
                                   in_=f_o.rearrange("(b p) d -> b p d", p=128)[
                                       phi, :, cq * 1024:(cq + 1) * 1024]))
                    S_w[ws].wrote(td)
                    deps = [td] + oT_ready
                    if phi == 0:
                        deps += bank_first_deps
                    def mm8(e, phi=phi, wap=wap):
                        last = None
                        for tau in range(4):
                            for g2 in range(2):
                                last = e.matmul(
                                    pb[tau * 2 + g2][:],
                                    t_oT.ap()[:, phi * 512 + tau * 128: phi * 512 + (tau + 1) * 128],
                                    wap[:, g2 * 512:(g2 + 1) * 512],
                                    start=(phi == 0), stop=False, skip_group_check=True)
                        return last
                    tmm = K.op("tensor", deps, mm8)
                    S_w[ws].read(tmm)
                    S_oT.read(tmm)
                    for b in range(8):
                        S_pb[b].wrote(tmm)
                def mmb(e, cq=cq):
                    last = None
                    for tau in range(4):
                        for g2 in range(2):
                            last = e.matmul(
                                pb[tau * 2 + g2][:], ones_sb[0:1, 0:128],
                                bo_bf.ap()[0:1, cq * 1024 + g2 * 512: cq * 1024 + (g2 + 1) * 512],
                                start=False, stop=True, skip_group_check=True)
                    return last
                tbias = K.op("tensor", ones_tok + S_bo.rdep(), mmb)
                S_bo.read(tbias)
                for b in range(8):
                    S_pb[b].wrote(tbias)
                for tau in range(4):
                    for g2 in range(2):
                        b = tau * 2 + g2
                        xs = b
                        col0 = cq * 1024 + g2 * 512
                        tx = K.dma("scalar", S_x[xs].wdep() + S_xtile.wdep(),
                                   lambda e, ti=ti, tau=tau, col0=col0, xs=xs: e.dma_start(
                                       out=t_x.ap()[:, xs * 512:(xs + 1) * 512],
                                       in_=x_in[ti * TT + tau * 128: ti * TT + (tau + 1) * 128,
                                                col0:col0 + 512]))
                        S_x[xs].wrote(tx)
                        tadd = K.op("vector", [tbias, tx],
                                    lambda e, b=b, xs=xs: e.tensor_add(
                                        t_x.ap()[:, xs * 512:(xs + 1) * 512],
                                        pb[b][:], t_x.ap()[:, xs * 512:(xs + 1) * 512]))
                        S_pb[b].read(tadd)
                        S_x[xs].wrote(tadd)
                        tst = K.dma("sync", [tadd] + S_r1hbm[ti][tau].rdep() +
                                    S_r1hbm[ti][tau].r,
                                    lambda e, ti=ti, tau=tau, col0=col0, xs=xs: e.dma_start(
                                        out=r1_hbm[ti * TT + tau * 128:
                                                   ti * TT + (tau + 1) * 128,
                                                   col0:col0 + 512],
                                        in_=t_x.ap()[:, xs * 512:(xs + 1) * 512]))
                        S_x[xs].read(tst)
                        S_r1hbm[ti][tau].r.append(tst)
            r1_all = K.queue_fence("sync")
            for tau in range(NTAU):
                S_r1hbm[ti][tau].wrote(r1_all)

        def ffn_tile(ti):
            xnT_toks = ln_and_transpose(ti, "ln2")
            inter_last = []
            for a in range(ICH_REAL):
                cs = a & 1
                wg_ap = t_oT.ap()[:, cs * 4096:(cs + 1) * 4096]
                wu_ap = t_oT.ap()[:, (2 + cs) * 4096:(3 + cs) * 4096]
                sg_ = _Slot
                tdg = K.dma("sync", S_w[4 + cs].wdep() + [AG["g"]] +
                            (S_oT.wdep() + SD_out[0].wdep() + SD_out[1].wdep()
                             if a < 2 else []),
                            lambda e, a=a, wg_ap=wg_ap: e.dma_start(
                                out=wg_ap, in_=f_g.rearrange("(a p) d -> a p d", p=128)[a]))
                S_w[4 + cs].wrote(tdg)
                tdu = K.dma("sync", S_w[6 + cs].wdep() + [AG["u"]],
                            lambda e, a=a, wu_ap=wu_ap: e.dma_start(
                                out=wu_ap, in_=f_u.rearrange("(a p) d -> a p d", p=128)[a]))
                S_w[6 + cs].wrote(tdu)
                bg, bu = cs, 2 + cs
                def chg(e, wg_ap=wg_ap, bg=bg):
                    last = None
                    for j in range(32):
                        last = e.matmul(pb[bg][:], wg_ap[:, j * 128:(j + 1) * 128],
                                        t_xnT.ap()[:, j * 512:(j + 1) * 512],
                                        start=(j == 0), stop=(j == 31))
                    return last
                tg = K.op("tensor", [tdg] + xnT_toks[-1:] + S_pb[bg].wdep(), chg)
                S_pb[bg].wrote(tg)
                S_w[4 + cs].read(tg)
                S_xnT.read(tg)
                def chu(e, wu_ap=wu_ap, bu=bu):
                    last = None
                    for j in range(32):
                        last = e.matmul(pb[bu][:], wu_ap[:, j * 128:(j + 1) * 128],
                                        t_xnT.ap()[:, j * 512:(j + 1) * 512],
                                        start=(j == 0), stop=(j == 31))
                    return last
                tu = K.op("tensor", [tdu] + xnT_toks[-1:] + S_pb[bu].wdep(), chu)
                S_pb[bu].wrote(tu)
                S_w[6 + cs].read(tu)
                S_xnT.read(tu)
                sgap = t_x.ap()[:, cs * 512:(cs + 1) * 512]
                tsg = K.op("scalar", [tg] + S_x[cs].wdep() + S_xtile.wdep(),
                           lambda e, bg=bg, sgap=sgap: e.activation(sgap, pb[bg][:], AF.Silu))
                S_x[cs].wrote(tsg)
                S_pb[bg].read(tsg)
                act_release(bg, tsg)
                tin = K.op("vector", [tsg, tu] + S_inter.wdep() +
                           (S_q.wdep() + S_k.wdep() + S_v.wdep() if a < 2 else []),
                           lambda e, a=a, bu=bu, sgap=sgap: e.tensor_tensor(
                               t_qkv.ap()[:, a * 512:(a + 1) * 512], sgap, pb[bu][:], ALU.mult))
                S_pb[bu].read(tin)
                S_x[cs].read(tin)
                S_inter.r.append(tin)
                inter_last.append(tin)
            inter_ready = inter_last[-1:]
            for cq in range(4):
                bank_deps = []
                for b in range(8):
                    bank_deps += S_pb[b].wdep()
                    S_pb[b].r = []
                    S_pb[b].w = None
                for a in range(ICH_REAL):
                    ws = a & 1
                    wap = t_w.ap()[:, ws * 1024:(ws + 1) * 1024]
                    td = K.dma("sync", S_w[ws].wdep() + [AG["d"]],
                               lambda e, a=a, cq=cq, wap=wap: e.dma_start(
                                   out=wap,
                                   in_=f_d.rearrange("(a p) d -> a p d", p=128)[
                                       a, :, cq * 1024:(cq + 1) * 1024]))
                    S_w[ws].wrote(td)
                    deps = [td, inter_last[a]]
                    if a == 0:
                        deps += bank_deps
                    def mmd(e, a=a, wap=wap):
                        last = None
                        for g2 in range(2):
                            for tau in range(4):
                                last = e.matmul(
                                    pb[g2 * 4 + tau][:],
                                    t_qkv.ap()[:, a * 512 + tau * 128: a * 512 + (tau + 1) * 128],
                                    wap[:, g2 * 512:(g2 + 1) * 512],
                                    start=(a == 0), stop=(a == ICH_REAL - 1),
                                    skip_group_check=True)
                        return last
                    tmm = K.op("tensor", deps, mmd)
                    S_inter.r.append(tmm)
                    S_w[ws].read(tmm)
                    for b in range(8):
                        S_pb[b].wrote(tmm)
                for g2 in range(2):
                    for tau in range(4):
                        b = g2 * 4 + tau
                        col0 = cq * 1024 + g2 * 512
                        xs = b
                        tr_ = K.dma("scalar", S_x[xs].wdep() + S_xtile.wdep() +
                                    S_r1hbm[ti][tau].rdep(),
                                    lambda e, ti=ti, tau=tau, col0=col0, xs=xs: e.dma_start(
                                        out=t_x.ap()[:, xs * 512:(xs + 1) * 512],
                                        in_=r1_hbm[ti * TT + tau * 128:
                                                   ti * TT + (tau + 1) * 128,
                                                   col0:col0 + 512]))
                        S_x[xs].wrote(tr_)
                        S_r1hbm[ti][tau].read(tr_)
                        tadd = K.op("vector", S_pb[b].rdep() + [tr_],
                                    lambda e, b=b, xs=xs: e.tensor_add(
                                        t_x.ap()[:, xs * 512:(xs + 1) * 512],
                                        pb[b][:], t_x.ap()[:, xs * 512:(xs + 1) * 512]))
                        S_pb[b].read(tadd)
                        S_x[xs].wrote(tadd)
                        tst = K.dma("scalar", [tadd],
                                    lambda e, ti=ti, tau=tau, col0=col0, xs=xs: e.dma_start(
                                        out=y_out[ti * TT + tau * 128: ti * TT + (tau + 1) * 128,
                                                  col0:col0 + 512],
                                        in_=t_x.ap()[:, xs * 512:(xs + 1) * 512]))
                        S_x[xs].read(tst)

        for ti in range(NTILE):
            blevels = [p for p in phases if p.startswith(f"b{ti}")]
            if blevels:
                lv = 4
                if ":" in blevels[0]:
                    lv = int(blevels[0].split(":")[1])
                attn_tile(ti, lv)
        for ti in range(NTILE):
            if f"c{ti}" in phases:
                ffn_tile(ti)

        fin = [K.final_token(s) for s in ("sT", "sV", "sS", "sP")]
        for base in ("sQ", "sR", "sG"):
            for i in range(_K.NLANE):
                fin.append(K.final_token(f"{base}{i}"))
        def drain(e):
            return e.nop()
        K.op("gpsimd", fin, drain)

        with nc.Block() as block:
            def mk(eng_name):
                def emitf(eng):
                    for waits, build, sem_h, amt in K.streams[eng_name]:
                        for sname, v in waits:
                            eng.wait_ge(K.sem_handles[sname], v)
                        inst = build(eng)
                        inst.then_inc(sem_h, amt)
                return emitf
            block.tensor(mk("tensor"))
            block.vector(mk("vector"))
            block.scalar(mk("scalar"))
            block.sync(mk("sync"))
            block.gpsimd(mk("gpsimd"))
    return nc


_NC_CACHE = None


def _get_nc():
    global _NC_CACHE
    if _NC_CACHE is None:
        _NC_CACHE = _build()
    return _NC_CACHE


def _codes16(c):
    c = np.ascontiguousarray(np.asarray(c), dtype="<i4")
    return np.ascontiguousarray(c.view("<i2")[:, ::2])


def _prep_inputs(L0_attn_x, L0_attn_Wq, L0_attn_bq, L0_attn_Wk, L0_attn_bk,
                 L0_attn_Wv, L0_attn_bv, L0_attn_Wo, L0_attn_bo,
                 L0_ffn_gate_codes, L0_ffn_gate_scales, L0_ffn_up_codes,
                 L0_ffn_up_scales, L0_ffn_down_codes, L0_ffn_down_scales):
    bf16 = ml_dtypes.bfloat16
    X = np.ascontiguousarray(np.asarray(L0_attn_x, dtype=np.float32).reshape(8192, D))

    def img_qkv(W):
        W = np.asarray(W, dtype=np.float32)
        return np.ascontiguousarray(
            W.reshape(NH, HD, 32, 128).transpose(0, 3, 2, 1).reshape(NH, 128, D))

    WQ, WK, WV = img_qkv(L0_attn_Wq), img_qkv(L0_attn_Wk), img_qkv(L0_attn_Wv)
    WO = np.ascontiguousarray(
        np.asarray(L0_attn_Wo, dtype=np.float32).T.reshape(NH, 128, D))
    BQKV = np.ascontiguousarray(np.concatenate(
        [np.asarray(b, dtype=np.float32).reshape(NH, HD).T for b in
         (L0_attn_bq, L0_attn_bk, L0_attn_bv)], axis=1))
    BO = np.ascontiguousarray(np.asarray(L0_attn_bo, dtype=np.float32).reshape(1, D))

    cg = np.pad(_codes16(L0_ffn_gate_codes), ((0, I_PAD - I_REAL), (0, 0)))
    cu = np.pad(_codes16(L0_ffn_up_codes), ((0, I_PAD - I_REAL), (0, 0)))
    cd = np.pad(_codes16(L0_ffn_down_codes), ((0, 0), (0, I_PAD - I_REAL)))
    sg = np.pad(np.ascontiguousarray(np.asarray(L0_ffn_gate_scales, np.float32).T),
                ((0, 0), (0, I_PAD - I_REAL)))
    su = np.pad(np.ascontiguousarray(np.asarray(L0_ffn_up_scales, np.float32).T),
                ((0, 0), (0, I_PAD - I_REAL)))
    sd = np.pad(np.ascontiguousarray(np.asarray(L0_ffn_down_scales, np.float32).T),
                ((0, I_PAD // 64 - I_REAL // 64), (0, 0)))

    IDENT = np.eye(128, dtype=bf16)
    BTHR = np.full((128, 128), 1e30, np.float32)
    for b in range(4):
        BTHR[b * 32:(b + 1) * 32, b * 32:(b + 1) * 32] = _THR_RAW
    PWLB = np.zeros((128, 16), np.float32)
    PWLB[:, 0] = _T0
    for j in range(1, 15):
        PWLB[:, j] = np.float32(-np.float64(_GAM32[j - 1]) * j)
    PWLB[:, 15] = _EPS
    ONES = np.ones((1, 128), dtype=bf16)

    in_maps = []
    for k in range(NCORES):
        in_maps.append({
            "x_in": np.ascontiguousarray(X[k * TOK:(k + 1) * TOK]),
            "wq_p": np.ascontiguousarray(WQ[4 * k:4 * k + 4]),
            "wk_p": np.ascontiguousarray(WK[4 * k:4 * k + 4]),
            "wv_p": np.ascontiguousarray(WV[4 * k:4 * k + 4]),
            "wo_p": np.ascontiguousarray(WO[4 * k:4 * k + 4]),
            "bqkv": BQKV,
            "bo_in": BO,
            "cg_in": np.ascontiguousarray(cg[k * SH_I:(k + 1) * SH_I]).view(bf16),
            "cu_in": np.ascontiguousarray(cu[k * SH_I:(k + 1) * SH_I]).view(bf16),
            "cd_in": np.ascontiguousarray(cd[:, k * SH_I:(k + 1) * SH_I]).view(bf16),
            "sg_in": np.ascontiguousarray(sg[:, k * SH_I:(k + 1) * SH_I]),
            "su_in": np.ascontiguousarray(su[:, k * SH_I:(k + 1) * SH_I]),
            "sd_in": np.ascontiguousarray(sd[22 * k:22 * k + 22]),
            "ident_in": IDENT,
            "bthr_in": BTHR,
            "pwlb_in": PWLB,
            "ones_in": ONES,
        })
    return in_maps


def kernel(**inputs):
    in_maps = _prep_inputs(**inputs)
    nc = _get_nc()
    res = run_bass_kernel_spmd(nc, in_maps, core_ids=list(range(NCORES)))
    y = np.concatenate([np.asarray(res.results[k]["y_out"], dtype=np.float32)
                        for k in range(NCORES)], axis=0)
    return y.reshape(1, 8192, D)
